# version 05
# baseline (speedup 1.0000x reference)
import numpy as np
import concourse.bass as bass
import concourse.mybir as mybir
from concourse.bass_utils import run_bass_kernel_spmd

F32 = mybir.dt.float32
BF16 = mybir.dt.bfloat16
AF = mybir.ActivationFunctionType
ALU = mybir.AluOpType
AX = mybir.AxisListType

NCORE = 8
B = 2
S = 2048
D = 1024
NH = 16
HD = 64
E = 8
DFF = 4096
NTILE = 32
NTT = 4
TTK = 8
FT = 32
IC = 8
NEG = -1.0e30

_STAGE = 3
_DBG = 0
_NE = E
_NTTR = NTT
_AB = list(range(B))
_LAST_RES = None


class Prog:
    def __init__(self):
        nc = bass.Bass("TRN2", target_bir_lowering=False)
        self.nc = nc

        def dram(name, shape, dtype, kind):
            return nc.dram_tensor(name, shape, dtype, kind=kind).ap()

        self.d_x = dram("x", [NTILE, 128, D], F32, "ExternalInput")
        self.d_y = dram("y", [NTILE, 128, D], F32, "ExternalOutput")
        self.d_w1 = dram("w1a", [E, FT, 128, IC, 128], F32, "ExternalInput")
        self.d_w2 = dram("w2a", [E, DFF, D], F32, "ExternalInput")
        self.d_qkw = dram("qkwa", [NH, 128, IC, 128], F32, "ExternalInput")
        self.d_vw = dram("vwa", [NH // 2, 128, IC, 128], F32, "ExternalInput")
        self.d_wo = dram("woa", [IC, 128, D], F32, "ExternalInput")
        self.d_b1 = dram("b1a", [128, E * FT], F32, "ExternalInput")
        self.d_b2 = dram("b2a", [E, 128, D], F32, "ExternalInput")
        self.d_wrx = dram("wrx", [128, IC * E], F32, "ExternalInput")
        self.d_br = dram("bra", [128, E], F32, "ExternalInput")
        self.d_tril = dram("trila", [128, 128], F32, "ExternalInput")
        if _DBG:
            self.d_dbg_c = dram("dbg_c", [128, NTILE * E], F32, "ExternalOutput")
            self.d_dbg_st = dram("dbg_st", [128, NTILE * 2], F32, "ExternalOutput")
            self.d_dbg_rt = dram("dbg_rt", [128, TTK * 32], F32, "ExternalOutput")
            self.d_dbg_h1 = dram("dbg_h1", [128, 8192], BF16, "ExternalOutput")
            self.d_dbg_xnt = dram("dbg_xnt", [128, 8192], BF16, "ExternalOutput")
            self.d_dbg_x = dram("dbg_x", [128, 4096], BF16, "ExternalOutput")
            self.d_dbg_w1 = dram("dbg_w1", [128, 4096], BF16, "ExternalOutput")
            self.d_dbg_yal = dram("dbg_yal", [128, 4096], BF16, "ExternalOutput")
            self.d_dbg_qk = dram("dbg_qk", [128, 4096], BF16, "ExternalOutput")
            self.d_dbg_pr = dram("dbg_pr", [128, 4096], BF16, "ExternalOutput")
            self.d_dbg_hw = dram("dbg_hw", [128, 4096], BF16, "ExternalOutput")
            self.d_dbg_v0 = dram("dbg_v0", [128, 2048], BF16, "ExternalOutput")

        sb = nc.alloc_sbuf_tensor
        self.x_sb = sb("x_sb", [128, NTILE * D], BF16)
        self.stat = sb("stat", [128, NTILE * 2], F32)
        self.c_sb = sb("c_sb", [128, NTILE * E], F32)
        self.b1_sb = sb("b1_sb", [128, E * FT], F32)
        self.br_sb = sb("br_sb", [128, E], F32)
        self.tril_sb = sb("tril_sb", [128, 128], F32)
        self.wr_sb = sb("wr_sb", [128, IC * E], BF16)
        self.sm = sb("sm", [128, 32], F32)
        self.rt = sb("rt", [128, TTK * 32], F32)
        self.a8 = sb("a8", [128, 64], F32)
        self.dump = sb("dump", [128, D], F32)
        self.ystage = sb("ystage", [128, 2 * D], F32)
        self.xn_nat = sb("xn_nat", [128, 2 * D], BF16)
        self.p0reg = sb("p0reg", [128, 2 * D], F32)
        self.b2buf = sb("b2buf", [128, 2 * D], F32)
        self.w1buf = sb("w1buf", [128, 4 * 1024], BF16)
        self.w2buf = sb("w2buf", [128, 8 * 512], BF16)
        self.arena = sb("arena", [128, 45120], BF16)

        self.yal = self.ystage.bitcast(BF16)
        self.hw = self.b2buf.bitcast(BF16)
        self.wo_a = self.p0reg.bitcast(BF16)

        self.FXNT = 0
        self.FH1 = 8192
        self.AXNT = 0
        self.AOT = 16384
        self.AQKN = 32768
        self.AQT = 34880
        self.AKT = 36928
        self.APR = 38976
        self.APT = 43072

        self.ps = nc.alloc_psum_tensor("ps", [128, 4096], F32)

        self.sem_h = {}
        self.sem_c = {}
        for s_ in ("mm", "sc", "vx", "gw", "tp", "sy"):
            self.sem_h[s_] = nc.alloc_semaphore(s_)
            self.sem_c[s_] = 0
        self.waited = {e: {} for e in ("tensor", "scalar", "vector", "sync", "gpsimd")}
        self.eng = None
        self.bank_writer = [None] * 8
        self.bank_readers = [[] for _ in range(8)]

    def inc(self, ins, sem, n=1):
        ins.then_inc(self.sem_h[sem], n)
        self.sem_c[sem] += n
        return (sem, self.sem_c[sem])

    def wait(self, engname, sem, cnt):
        if cnt <= 0:
            return
        if self.waited[engname].get(sem, -1) >= cnt:
            return
        self.eng[engname].wait_ge(self.sem_h[sem], cnt)
        self.waited[engname][sem] = cnt

    def wmk(self, engname, marks):
        for m in marks:
            if m is not None:
                self.wait(engname, m[0], m[1])

    def vsync(self, ins):
        mk = self.inc(ins, "vx")
        self.wmk("vector", [mk])
        return mk

    def ssync(self, ins):
        mk = self.inc(ins, "sc")
        self.wmk("scalar", [mk])
        return mk

    def t_banks(self, banks):
        for b in banks:
            for mk in self.bank_readers[b]:
                self.wait("tensor", mk[0], mk[1])
            self.bank_readers[b] = []

    def t_close(self, ins, banks):
        mk = self.inc(ins, "mm")
        for b in banks:
            self.bank_writer[b] = mk
        return mk

    def ev_wait(self, engname, b):
        w = self.bank_writer[b]
        assert w is not None, f"bank {b} never written"
        self.wait(engname, w[0], w[1])

    def ev_done(self, ins, engname, banks):
        sem = {"scalar": "sc", "vector": "vx"}[engname]
        mk = self.inc(ins, sem)
        if isinstance(banks, int):
            banks = [banks]
        for b in banks:
            self.bank_readers[b].append(mk)
        return mk

    def build(self):
        nc = self.nc
        with nc.Block() as block:
            @block.sync
            def _(_sync):
                self.eng = {
                    "tensor": nc.tensor, "scalar": nc.scalar,
                    "vector": nc.vector, "sync": nc.sync, "gpsimd": nc.gpsimd,
                }
                self.emit()
        return nc

    def stats_tile(self, t, extra_wait=()):
        nc = self.nc
        slot = t % 2
        base = slot * 16
        xs = self.x_sb.ap()[:, t * D:(t + 1) * D]
        st_ = self.stat.ap()
        smv = self.sm.ap()
        c_s1 = smv[:, base + 0:base + 1]
        c_ss = smv[:, base + 1:base + 2]
        c_mean = smv[:, base + 2:base + 3]
        c_tv = smv[:, base + 3:base + 4]
        c_m2 = smv[:, base + 4:base + 5]
        c_v2 = smv[:, base + 5:base + 6]
        c_ln = smv[:, base + 6:base + 7]
        c_tmp = smv[:, base + 7:base + 8]
        c_v2a = smv[:, base + 8:base + 9]
        self.wmk("vector", list(extra_wait))
        if self.sm_slot_free[slot] is not None:
            self.wmk("vector", [self.sm_slot_free[slot]])
        nc.vector.reduce_sum(c_s1, xs, axis=AX.X)
        nc.vector.tensor_mul(self.dump.ap()[:, 0:D], xs, xs)
        nc.vector.reduce_sum(c_ss, self.dump.ap()[:, 0:D], axis=AX.X)
        nc.vector.tensor_scalar(c_mean, c_s1, 1.0 / D, None, ALU.mult)
        nc.vector.tensor_scalar(c_tv, c_ss, 1.0 / D, None, ALU.mult)
        self.vsync(nc.vector.tensor_mul(c_m2, c_mean, c_mean))
        self.vsync(nc.vector.tensor_sub(c_v2a, c_tv, c_m2))
        ins = nc.vector.tensor_scalar(c_v2, c_v2a, 1e-5, None, ALU.add)
        mk_v = self.inc(ins, "vx")
        self.wmk("scalar", [mk_v])
        nc.scalar.activation(c_ln, c_v2, AF.Ln)
        ins = nc.scalar.activation(st_[:, 2 * t:2 * t + 1], c_ln, AF.Exp, scale=-0.5)
        mk_s = self.inc(ins, "sc")
        self.wmk("vector", [mk_s])
        self.vsync(nc.vector.tensor_mul(c_tmp, c_mean, st_[:, 2 * t:2 * t + 1]))
        ins = nc.vector.tensor_scalar(st_[:, 2 * t + 1:2 * t + 2], c_tmp, -1.0,
                                      None, ALU.mult)
        mk_done = self.inc(ins, "vx")
        self.sm_slot_free[slot] = mk_done
        self.stat_ready[t] = mk_done

    def emit(self):
        nc = self.nc
        self.sm_slot_free = [None, None]
        self.stat_ready = [None] * NTILE
        self.x_final = [None] * NTILE
        self.w1_slot_free = [None] * 4
        self.w2_slot_free = [None] * 8
        self.b2_slot_free = [None, None]
        self.xn_slot_free = [None, None]
        self.yst_slot_free = [None, None]
        self.a8_free = {}
        self.c_ready = {}
        self.rt_free = None
        self.xnt_used = None
        self.axnt_used = None
        self.aot_used = None
        self.probs_free = [None, None]
        self.probsT_used = None
        self.onat_free = None
        self.vsb_free = None
        self.qkn_free = None
        self.qkt_used = None
        self.qkw_free = [None, None]
        self.vw_free = [None, None]
        self._last_A = None
        self._last_B = None
        self._last_lg = None
        self.mk_xnt_cur = None

        ins = nc.sync.dma_start(out=self.b1_sb.ap()[:], in_=self.d_b1)
        self.inc(ins, "sy", 16)
        ins = nc.sync.dma_start(out=self.br_sb.ap()[:], in_=self.d_br)
        self.inc(ins, "sy", 16)
        ins = nc.sync.dma_start(out=self.tril_sb.ap()[:], in_=self.d_tril)
        self.mk_consts = self.inc(ins, "sy", 16)
        ins = nc.gpsimd.dma_start(out=self.wr_sb.ap()[:], in_=self.d_wrx)
        self.inc(ins, "gw", 16)

        for t in range(NTILE):
            ins = nc.gpsimd.dma_start(out=self.x_sb.ap()[:, t * D:(t + 1) * D],
                                      in_=self.d_x[t])
            self.inc(ins, "gw", 16)
        mk_loads = ("gw", self.sem_c["gw"])
        self.mk_wr = mk_loads
        for t in range(NTILE):
            self.x_final[t] = mk_loads

        self.ffn_call()
        if _STAGE >= 2:
            self.attn()
        if _STAGE >= 3:
            self.ffn_call()

        if _DBG:
            for s_ in ("mm", "sc", "vx", "gw", "tp"):
                self.wait("sync", s_, self.sem_c[s_])
            ar = self.arena.ap()
            ins = nc.sync.dma_start(out=self.d_dbg_c, in_=self.c_sb.ap()[:])
            self.inc(ins, "sy", 16)
            ins = nc.sync.dma_start(out=self.d_dbg_st, in_=self.stat.ap()[:])
            self.inc(ins, "sy", 16)
            ins = nc.sync.dma_start(out=self.d_dbg_rt, in_=self.rt.ap()[:])
            self.inc(ins, "sy", 16)
            ins = nc.sync.dma_start(out=self.d_dbg_h1,
                                    in_=ar[:, self.FH1:self.FH1 + 8192])
            self.inc(ins, "sy", 16)
            ins = nc.sync.dma_start(out=self.d_dbg_xnt,
                                    in_=ar[:, self.FXNT:self.FXNT + 8192])
            self.inc(ins, "sy", 16)
            ins = nc.sync.dma_start(out=self.d_dbg_x,
                                    in_=self.x_sb.ap()[:, 0:4096])
            self.inc(ins, "sy", 16)
            ins = nc.sync.dma_start(out=self.d_dbg_w1, in_=self.w1buf.ap()[:])
            self.inc(ins, "sy", 16)
            ins = nc.sync.dma_start(out=self.d_dbg_yal, in_=self.yal.ap()[:])
            self.inc(ins, "sy", 16)
            ins = nc.sync.dma_start(
                out=self.d_dbg_qk, in_=ar[:, self.AQT:self.AQT + 4096])
            self.inc(ins, "sy", 16)
            ins = nc.sync.dma_start(
                out=self.d_dbg_pr, in_=ar[:, self.APR:self.APR + 4096])
            self.inc(ins, "sy", 16)
            ins = nc.sync.dma_start(out=self.d_dbg_hw, in_=self.hw.ap()[:])
            self.inc(ins, "sy", 16)

        if _STAGE >= 2:
            self.wmk("scalar", [self.onat_free])
        for t in range(NTILE):
            slot = t % 2
            ys = self.ystage.ap()[:, slot * D:(slot + 1) * D]
            self.wmk("scalar", [self.x_final[t], self.yst_slot_free[slot]])
            ins = nc.scalar.copy(ys, self.x_sb.ap()[:, t * D:(t + 1) * D])
            mk = self.inc(ins, "sc")
            self.wmk("sync", [mk])
            ins = nc.sync.dma_start(out=self.d_y[t], in_=ys)
            self.yst_slot_free[slot] = self.inc(ins, "sy", 16)
        self.wait("sync", "sy", self.sem_c["sy"])

    def ffn_call(self):
        for t in range(NTILE):
            self.stats_tile(t, extra_wait=[self.x_final[t]])
        for tt in range(_NTTR):
            self.ffn_chunk(tt)

    def router_chunk(self, tt):
        nc = self.nc
        ar = self.arena.ap()
        rt = self.rt.ap()
        ps = self.ps.ap()
        self.t_banks([7])
        self.wmk("tensor", [self.mk_wr, self.mk_xnt_cur])
        last = None
        for k in range(TTK):
            for ic in range(IC):
                lhsT = ar[:, self.FXNT + ic * 1024 + k * 128:
                          self.FXNT + ic * 1024 + k * 128 + 128]
                rhs = self.wr_sb.ap()[:, ic * E:(ic + 1) * E]
                last = nc.tensor.matmul(ps[:, 3584 + k * E:3584 + (k + 1) * E],
                                        lhsT, rhs, start=(ic == 0),
                                        stop=(ic == IC - 1))
        gm = self.t_close(last, [7])
        self.ev_wait("vector", 7)
        self.wmk("vector", [self.mk_consts, self.rt_free])
        last_v = None
        for k in range(TTK):
            last_v = nc.vector.tensor_add(rt[:, k * 32:k * 32 + E],
                                          ps[:, 3584 + k * E:3584 + (k + 1) * E],
                                          self.br_sb.ap()[:])
        mk_evac = self.inc(last_v, "vx")
        self.bank_readers[7].append(mk_evac)
        self.wmk("vector", [mk_evac])
        for k in range(TTK):
            last_v = nc.vector.reduce_max(rt[:, k * 32 + 24:k * 32 + 25],
                                          rt[:, k * 32:k * 32 + E],
                                          axis=AX.X, negate=True)
        mk_nm = self.inc(last_v, "vx")
        self.wmk("scalar", [mk_nm])
        last_s = None
        for k in range(TTK):
            nc.scalar.activation(rt[:, k * 32 + 8:k * 32 + 16],
                                 rt[:, k * 32:k * 32 + E], AF.Exp,
                                 bias=rt[:, k * 32 + 24:k * 32 + 25],
                                 accum_out=rt[:, k * 32 + 25:k * 32 + 26])
            last_s = nc.scalar.sign(rt[:, k * 32 + 16:k * 32 + 24],
                                    rt[:, k * 32:k * 32 + E],
                                    bias=rt[:, k * 32 + 24:k * 32 + 25])
        mk_s = self.inc(last_s, "sc")
        self.wmk("vector", [mk_s])
        for k in range(TTK):
            last_v = nc.vector.tensor_scalar(rt[:, k * 32 + 26:k * 32 + 27],
                                             rt[:, k * 32 + 25:k * 32 + 26],
                                             2.0, None, ALU.mult)
        self.wmk("vector", [self.inc(last_v, "vx")])
        for k in range(TTK):
            last_v = nc.vector.reciprocal(rt[:, k * 32 + 27:k * 32 + 28],
                                          rt[:, k * 32 + 26:k * 32 + 27])
        self.wmk("vector", [self.inc(last_v, "vx")])
        for k in range(TTK):
            t = tt * TTK + k
            last_v = nc.vector.tensor_scalar(
                self.c_sb.ap()[:, t * E:(t + 1) * E],
                rt[:, k * 32 + 16:k * 32 + 24], 1.0,
                rt[:, k * 32 + 27:k * 32 + 28], ALU.add, ALU.mult)
        mk_c = self.inc(last_v, "vx")
        self.c_ready[tt] = mk_c
        self.rt_free = mk_c

    def ffn_chunk(self, tt):
        nc = self.nc
        ar = self.arena.ap()
        mk_xnt = None
        for k in range(TTK):
            t = tt * TTK + k
            slot = t % 2
            xn = self.xn_nat.ap()[:, slot * D:(slot + 1) * D]
            self.wmk("vector", [self.stat_ready[t], self.xn_slot_free[slot]])
            st_ = self.stat.ap()
            ins = nc.vector.tensor_scalar(xn, self.x_sb.ap()[:, t * D:(t + 1) * D],
                                          st_[:, 2 * t:2 * t + 1],
                                          st_[:, 2 * t + 1:2 * t + 2],
                                          ALU.mult, ALU.add)
            mk = self.inc(ins, "vx")
            self.wmk("sync", [mk, self.xnt_used])
            dst = ar[:, self.FXNT:self.FXNT + IC * 1024].rearrange(
                "p (ic ttk c) -> p ic ttk c", ic=IC, ttk=TTK)[:, :, k, :]
            ins = nc.sync.dma_start(out=dst, in_=xn, transpose=True)
            mk_xnt = self.inc(ins, "tp", 16)
            self.xn_slot_free[slot] = mk_xnt
        self.mk_xnt_cur = mk_xnt

        self.router_chunk(tt)

        def load_w1_pair(e_, ft0):
            s0 = ft0 % 4
            self.wmk("gpsimd", [self.w1_slot_free[s0], self.w1_slot_free[s0 + 1]])
            dst = self.w1buf.ap()[:, s0 * 1024:(s0 + 2) * 1024]
            ins = nc.gpsimd.dma_start(
                out=dst,
                in_=self.d_w1[e_, ft0:ft0 + 2].rearrange("a p ic c -> p a ic c"))
            return self.inc(ins, "gw", 16)

        def load_w2_quad(e_, g0):
            hh_, f0 = divmod(g0, FT)
            s4 = g0 % 8
            self.wmk("gpsimd", [self.w2_slot_free[s4 + i] for i in range(4)])
            dst = self.w2buf.ap()[:, s4 * 512:(s4 + 4) * 512]
            ins = nc.gpsimd.dma_start(
                out=dst,
                in_=self.d_w2[e_, f0 * 128:(f0 + 4) * 128,
                              hh_ * 512:(hh_ + 1) * 512].rearrange(
                    "(a p) c -> p a c", p=128))
            return self.inc(ins, "gw", 16)

        for e in range(_NE):
            w1marks = [None] * FT
            for ft0 in (0, 2):
                mk = load_w1_pair(e, ft0)
                w1marks[ft0] = mk
                w1marks[ft0 + 1] = mk
            w2marks = {}
            for g0 in (0, 4):
                mk = load_w2_quad(e, g0)
                for i in range(4):
                    w2marks[g0 + i] = mk
            b2s = e % 2
            self.wmk("gpsimd", [self.b2_slot_free[b2s]])
            ins = nc.gpsimd.dma_start(out=self.b2buf.ap()[:, b2s * D:(b2s + 1) * D],
                                      in_=self.d_b2[e])
            mk_b2 = self.inc(ins, "gw", 16)

            h1_ready = [None] * FT
            for ft in range(FT):
                self.wmk("tensor", [w1marks[ft]])
                if ft == 0:
                    self.wmk("tensor", [mk_xnt])
                w1sl = self.w1buf.ap()[:, (ft % 4) * 1024:(ft % 4 + 1) * 1024]
                for nh2 in range(2):
                    bank = (ft * 2 + nh2) % 2
                    self.t_banks([bank])
                    pslice = self.ps.ap()[:, bank * 512:(bank + 1) * 512]
                    last = None
                    for ic in range(IC):
                        rhs = ar[:, self.FXNT + ic * 1024 + nh2 * 512:
                                 self.FXNT + ic * 1024 + nh2 * 512 + 512]
                        last = nc.tensor.matmul(pslice, w1sl[:, ic * 128:(ic + 1) * 128],
                                                rhs, start=(ic == 0), stop=(ic == IC - 1))
                    gm = self.t_close(last, [bank])
                    if nh2 == 1:
                        self.w1_slot_free[ft % 4] = gm
                        if ft == FT - 1:
                            self._last_A = gm
                        if ft % 2 == 1 and ft + 3 < FT:
                            mk = load_w1_pair(e, ft + 3)
                            w1marks[ft + 3] = mk
                            w1marks[ft + 4] = mk
                    self.ev_wait("scalar", bank)
                    dst = ar[:, self.FH1 + ft * 1024 + nh2 * 512:
                             self.FH1 + ft * 1024 + nh2 * 512 + 512]
                    ins = nc.scalar.activation(
                        dst, pslice, AF.Relu,
                        bias=self.b1_sb.ap()[:, e * FT + ft:e * FT + ft + 1])
                    mk = self.ev_done(ins, "scalar", bank)
                    if nh2 == 1:
                        h1_ready[ft] = mk

            for hh in range(2):
                for f in range(FT):
                    g = hh * FT + f
                    s_ = g % 8
                    self.wmk("tensor", [w2marks[g], h1_ready[f]])
                    w2sl = self.w2buf.ap()[:, s_ * 512:(s_ + 1) * 512]
                    if f == 0:
                        self.t_banks(list(range(8)))
                    last = None
                    for t8 in range(TTK):
                        lhs = ar[:, self.FH1 + f * 1024 + t8 * 128:
                                 self.FH1 + f * 1024 + t8 * 128 + 128]
                        pslice = self.ps.ap()[:, t8 * 512:(t8 + 1) * 512]
                        last = nc.tensor.matmul(pslice, lhs, w2sl,
                                                start=(f == 0), stop=(f == FT - 1))
                    gm = self.t_close(last, list(range(8)) if f == FT - 1 else [])
                    self.w2_slot_free[s_] = gm
                    if g % 4 == 3 and g + 5 + 3 < 2 * FT:
                        mk = load_w2_quad(e, g + 5)
                        for i in range(4):
                            w2marks[g + 5 + i] = mk
                    if f == FT - 1:
                        self._last_B = gm
                last_v = None
                for t8 in range(TTK):
                    t = tt * TTK + t8
                    self.ev_wait("vector", t8)
                    self.wmk("vector", [mk_b2, self.c_ready[tt]])
                    pslice = self.ps.ap()[:, t8 * 512:(t8 + 1) * 512]
                    xsl = self.x_sb.ap()[:, t * D + hh * 512:t * D + hh * 512 + 512]
                    ccol = self.c_sb.ap()[:, t * E + e:t * E + e + 1]
                    nc.vector.scalar_tensor_tensor(xsl, pslice, ccol, xsl,
                                                   ALU.mult, ALU.add)
                    b2sl = self.b2buf.ap()[:, b2s * D + hh * 512:b2s * D + hh * 512 + 512]
                    ins = nc.vector.scalar_tensor_tensor(xsl, b2sl, ccol, xsl,
                                                         ALU.mult, ALU.add)
                    mk = self.ev_done(ins, "vector", t8)
                    last_v = mk
                    if hh == 1 and e == _NE - 1:
                        self.x_final[t] = mk
                if hh == 1:
                    self.b2_slot_free[b2s] = last_v

        self.xnt_used = self._last_A

    def attn(self):
        nc = self.nc
        for t in range(NTILE):
            self.stats_tile(t, extra_wait=[self.x_final[t]])

        self.wmk("gpsimd", self.w1_slot_free + self.w2_slot_free)
        ins = nc.gpsimd.dma_start(
            out=self.wo_a.ap()[:, 0:4096].rearrange("p (a d) -> p a d", d=D),
            in_=self.d_wo[0:4].rearrange("a p d -> p a d"))
        mk_wo0 = self.inc(ins, "gw", 16)
        ins = nc.gpsimd.dma_start(
            out=self.w1buf.ap()[:, 0:4096].rearrange("p (a d) -> p a d", d=D),
            in_=self.d_wo[4:8].rearrange("a p d -> p a d"))
        mk_wo1 = self.inc(ins, "gw", 16)
        self.wo_marks = [mk_wo0, mk_wo1]
        self.axnt_used = self._last_B

        self._attn_first = True
        for b in _AB:
            self.attn_batch(b)
            self._attn_first = False

        self.xnt_used = self._attn_last_mm
        for s_ in range(4):
            self.w1_slot_free[s_] = self._attn_last_mm
        for s_ in range(8):
            self.w2_slot_free[s_] = self._attn_last_mm
        self.b2_slot_free = [self._attn_last_mm, self._attn_last_mm]

    def attn_batch(self, b):
        nc = self.nc
        ar = self.arena.ap()
        mk_xntb = None
        for k in range(16):
            t = b * 16 + k
            slot = t % 2
            xn = self.xn_nat.ap()[:, slot * D:(slot + 1) * D]
            self.wmk("vector", [self.stat_ready[t], self.x_final[t],
                                self.xn_slot_free[slot]])
            st_ = self.stat.ap()
            ins = nc.vector.tensor_scalar(xn, self.x_sb.ap()[:, t * D:(t + 1) * D],
                                          st_[:, 2 * t:2 * t + 1],
                                          st_[:, 2 * t + 1:2 * t + 2],
                                          ALU.mult, ALU.add)
            mk = self.inc(ins, "vx")
            self.wmk("sync", [mk, self.xnt_used, self.axnt_used])
            dst = ar[:, self.AXNT:self.AXNT + IC * 2048].rearrange(
                "p (ic kk c) -> p ic kk c", ic=IC, kk=16)[:, :, k, :]
            ins = nc.sync.dma_start(out=dst, in_=xn, transpose=True)
            mk_xntb = self.inc(ins, "tp", 16)
            self.xn_slot_free[slot] = mk_xntb
        self.xnt_used = None

        last_mm = None
        for hp in range(NH // 2):
            vs = hp % 2
            self.wmk("gpsimd", [self.vw_free[vs]])
            if hp == 0 and self._attn_first:
                self.wmk("gpsimd", [self.b2_slot_free[0], self.b2_slot_free[1]])
            ins = nc.gpsimd.dma_start(
                out=self.hw.ap()[:, 2048 + vs * 1024:2048 + (vs + 1) * 1024],
                in_=self.d_vw[hp].rearrange("p ic c -> p (ic c)"))
            mk_vw = self.inc(ins, "gw", 16)
            qk_marks = []
            for hh2 in range(2):
                h = 2 * hp + hh2
                qs = h % 2
                self.wmk("gpsimd", [self.qkw_free[qs]])
                ins = nc.gpsimd.dma_start(
                    out=self.hw.ap()[:, qs * 1024:(qs + 1) * 1024],
                    in_=self.d_qkw[h].rearrange("p ic c -> p (ic c)"))
                qk_marks.append(self.inc(ins, "gw", 16))

            v_ev = None
            for kt in range(16):
                bank = 7
                colA = 3584 + (kt % 2) * 128
                self.t_banks([bank])
                self.wmk("tensor", [mk_xntb, mk_vw])
                pv = self.ps.ap()[:, colA:colA + 128]
                last = None
                for ic in range(IC):
                    lhs = ar[:, self.AXNT + ic * 2048 + kt * 128:
                             self.AXNT + ic * 2048 + kt * 128 + 128]
                    rhs = self.hw.ap()[:, 2048 + vs * 1024 + ic * 128:
                                       2048 + vs * 1024 + (ic + 1) * 128]
                    last = nc.tensor.matmul(pv, lhs, rhs, start=(ic == 0),
                                            stop=(ic == IC - 1))
                gm = self.t_close(last, [bank])
                if kt == 15:
                    self.vw_free[vs] = gm
                self.ev_wait("scalar", bank)
                self.wmk("scalar", [self.vsb_free])
                ins = nc.scalar.copy(self.yal.ap()[:, kt * 128:(kt + 1) * 128], pv)
                v_ev = self.ev_done(ins, "scalar", bank)
            mk_vsb = v_ev
            self.vsb_free = None
            if _DBG and b == 1 and hp == 0:
                self.wmk("sync", [mk_vsb])
                ins = nc.sync.dma_start(out=self.d_dbg_v0,
                                        in_=self.yal.ap()[:, 0:2048])
                self.inc(ins, "sy", 16)

            onat_evs = []
            for hh2 in range(2):
                h = 2 * hp + hh2
                qs = h % 2
                qk_ev = None
                for kt in range(16):
                    bank = 6
                    colA = 3072 + (kt % 2) * 128
                    self.t_banks([bank])
                    self.wmk("tensor", [qk_marks[hh2]])
                    pv = self.ps.ap()[:, colA:colA + 128]
                    last = None
                    for ic in range(IC):
                        lhs = ar[:, self.AXNT + ic * 2048 + kt * 128:
                                 self.AXNT + ic * 2048 + kt * 128 + 128]
                        rhs = self.hw.ap()[:, qs * 1024 + ic * 128:
                                           qs * 1024 + (ic + 1) * 128]
                        last = nc.tensor.matmul(pv, lhs, rhs, start=(ic == 0),
                                                stop=(ic == IC - 1))
                    gm = self.t_close(last, [bank])
                    if kt == 15:
                        self.qkw_free[qs] = gm
                    self.ev_wait("scalar", bank)
                    self.wmk("scalar", [self.qkn_free])
                    nc.scalar.mul(ar[:, self.AQKN + kt * 128:self.AQKN + kt * 128 + 64],
                                  pv[:, 0:64], 0.125)
                    ins = nc.scalar.copy(
                        ar[:, self.AQKN + kt * 128 + 64:self.AQKN + (kt + 1) * 128],
                        pv[:, 64:128])
                    qk_ev = self.ev_done(ins, "scalar", bank)
                self.wmk("sync", [qk_ev, self.qkt_used])
                dstq = ar[:, self.AQT:self.AQT + 2048].rearrange("p (a c) -> p a c", c=128)
                ins = nc.sync.dma_start(out=dstq, in_=ar[:, self.AQKN:self.AQKN + 2048],
                                        transpose=True)
                self.inc(ins, "tp", 16)
                dstk = ar[:, self.AKT:self.AKT + 2048].rearrange("p (a c) -> p a c", c=128)
                ins = nc.sync.dma_start(
                    out=dstk, in_=ar[:, self.AQKN + 64:self.AQKN + 64 + 2048],
                    transpose=True)
                mk_qkt = self.inc(ins, "tp", 16)
                self.qkn_free = mk_qkt

                for st in range(16):
                    ev = self.attn_stile(h, hh2, st, mk_qkt, mk_vsb)
                    onat_evs.append(ev)
                self.qkt_used = self._last_lg

            self.wmk("sync", onat_evs[-32:])
            if not self._attn_first:
                self.wmk("sync", [self.aot_used])
            dst = ar[:, self.AOT + hp * 2048:self.AOT + (hp + 1) * 2048].rearrange(
                "p (a c) -> p a c", c=128)
            ins = nc.sync.dma_start(out=dst, in_=self.yal.ap()[:, 2048:4096],
                                    transpose=True)
            self.onat_free = self.inc(ins, "tp", 16)
            self.oT_marks = getattr(self, "oT_marks", [])
            if hp == 0:
                self.oT_marks = []
            self.oT_marks.append(self.onat_free)
            self.vsb_free = self.onat_free

        self.wmk("tensor", self.wo_marks + self.oT_marks)
        for k in range(16):
            t = b * 16 + k
            for no in range(2):
                bank = no
                self.t_banks([bank])
                pslice = self.ps.ap()[:, bank * 512:(bank + 1) * 512]
                last = None
                for ic in range(IC):
                    lhs = ar[:, self.AOT + ic * 2048 + k * 128:
                             self.AOT + ic * 2048 + k * 128 + 128]
                    if ic < 4:
                        rhs = self.wo_a.ap()[:, ic * 1024 + no * 512:
                                             ic * 1024 + no * 512 + 512]
                    else:
                        rhs = self.w1buf.ap()[:, (ic - 4) * 1024 + no * 512:
                                              (ic - 4) * 1024 + no * 512 + 512]
                    last = nc.tensor.matmul(pslice, lhs, rhs, start=(ic == 0),
                                            stop=(ic == IC - 1))
                gm = self.t_close(last, [bank])
                last_mm = gm
                self.ev_wait("vector", bank)
                xsl = self.x_sb.ap()[:, t * D + no * 512:t * D + no * 512 + 512]
                ins = nc.vector.tensor_add(xsl, xsl, pslice)
                mk = self.ev_done(ins, "vector", bank)
                if no == 1:
                    self.x_final[t] = mk
        self.axnt_used = last_mm
        self.aot_used = last_mm
        self._attn_last_mm = last_mm

    def attn_stile(self, h, hh2, st, mk_qkt, mk_vsb):
        nc = self.nc
        ar = self.arena.ap()
        ps = self.ps.ap()
        tv = (st + 1) * 128
        R0 = 1536 * (st % 2)
        sl = (st % 2) * 32
        a = self.a8.ap()
        nm = [a[:, sl + 0:sl + 1], a[:, sl + 1:sl + 2]]
        Zc = [a[:, sl + 4:sl + 5], a[:, sl + 5:sl + 6]]
        fc = [a[:, sl + 6:sl + 7], a[:, sl + 7:sl + 8]]
        nmt = a[:, sl + 8:sl + 9]
        Zt = a[:, sl + 10:sl + 11]
        rz = a[:, sl + 11:sl + 12]
        sc0 = a[:, sl + 12:sl + 13]
        sc1 = a[:, sl + 13:sl + 14]
        d0 = a[:, sl + 14:sl + 15]
        d1 = a[:, sl + 15:sl + 16]

        widths = [tv] if tv <= 1536 else [1536, tv - 1536]
        pslot = st % 2
        probs = ar[:, self.APR + pslot * 2048:self.APR + pslot * 2048 + 2048]
        self.wmk("scalar", [self.probs_free[pslot]])
        self.wmk("vector", [self.a8_free.get(st % 2)])

        exp_marks = []
        for c, w in enumerate(widths):
            t0 = c * 1536
            lg_banks = []
            nblk = (w + 511) // 512
            for tb in range(nblk):
                bw = min(512, w - tb * 512)
                col = R0 + tb * 512
                bank = col // 512
                self.t_banks([bank])
                self.wmk("tensor", [mk_qkt])
                lhs = ar[0:64, self.AQT + st * 128:self.AQT + (st + 1) * 128]
                rhs = ar[0:64, self.AKT + t0 + tb * 512:self.AKT + t0 + tb * 512 + bw]
                ins = nc.tensor.matmul(ps[:, col:col + bw], lhs, rhs,
                                       start=True, stop=True)
                gm = self.t_close(ins, [bank])
                lg_banks.append(bank)
                self._last_lg = gm
            diag0 = st * 128
            if t0 <= diag0 < t0 + w:
                lc = R0 + (diag0 - t0)
                bank = lc // 512
                self.ev_wait("vector", bank)
                pv = ps[:, lc:lc + 128]
                ins = nc.vector.tensor_add(pv, pv, self.tril_sb.ap()[:])
                mk_d = self.inc(ins, "vx")
                self.bank_readers[bank].append(mk_d)
                self.wmk("vector", [mk_d])
            for bank in lg_banks:
                self.ev_wait("vector", bank)
            ins = nc.vector.reduce_max(nm[c], ps[:, R0:R0 + w], axis=AX.X,
                                       negate=True)
            mk_v = self.inc(ins, "vx")
            self.wmk("scalar", [mk_v])
            ins = nc.scalar.activation(probs[:, t0:t0 + w], ps[:, R0:R0 + w],
                                       AF.Exp, bias=nm[c], accum_out=Zc[c])
            mk_e = self.inc(ins, "sc")
            exp_marks.append(mk_e)
            for bank in lg_banks:
                self.bank_readers[bank].append(mk_e)

        self.wmk("vector", exp_marks)
        if len(widths) == 1:
            ins = nc.vector.reciprocal(rz, Zc[0])
            mk_n = self.inc(ins, "vx")
            probs_mark = exp_marks[-1]
        else:
            self.vsync(nc.vector.tensor_tensor(nmt, nm[0], nm[1], ALU.min))
            nc.vector.tensor_sub(d0, nmt, nm[0])
            ins = nc.vector.tensor_sub(d1, nmt, nm[1])
            mk_v = self.inc(ins, "vx")
            self.wmk("scalar", [mk_v])
            nc.scalar.activation(fc[0], d0, AF.Exp)
            ins = nc.scalar.activation(fc[1], d1, AF.Exp)
            mk_s = self.inc(ins, "sc")
            self.wmk("vector", [mk_s])
            nc.vector.tensor_scalar_mul(probs[:, 0:1536], probs[:, 0:1536], fc[0])
            ins = nc.vector.tensor_scalar_mul(probs[:, 1536:tv], probs[:, 1536:tv],
                                              fc[1])
            probs_mark = self.inc(ins, "vx")
            nc.vector.tensor_mul(sc0, fc[0], Zc[0])
            self.vsync(nc.vector.tensor_mul(sc1, fc[1], Zc[1]))
            self.vsync(nc.vector.tensor_add(Zt, sc0, sc1))
            ins = nc.vector.reciprocal(rz, Zt)
            mk_n = self.inc(ins, "vx")

        self.wmk("sync", [probs_mark, self.probsT_used])
        dst = ar[:, self.APT:self.APT + tv].rearrange("p (a c) -> p a c", c=128)
        ins = nc.sync.dma_start(out=dst, in_=probs[:, 0:tv], transpose=True)
        mk_t = self.inc(ins, "tp", 16)
        self.probs_free[pslot] = mk_t

        bank = 6
        col = 3072 + hh2 * 64
        self.t_banks([bank])
        self.wmk("tensor", [mk_t, mk_vsb])
        po = ps[:, col:col + 64]
        last = None
        for kt in range(st + 1):
            lhs = ar[:, self.APT + kt * 128:self.APT + (kt + 1) * 128]
            rhs = self.yal.ap()[:, kt * 128 + hh2 * 64:kt * 128 + hh2 * 64 + 64]
            last = nc.tensor.matmul(po, lhs, rhs, start=(kt == 0), stop=(kt == st))
        gm = self.t_close(last, [bank])
        self.probsT_used = gm
        self.ev_wait("scalar", bank)
        self.wmk("scalar", [self.onat_free, mk_n])
        dst = self.yal.ap()[:, 2048 + st * 128 + hh2 * 64:2048 + st * 128 + hh2 * 64 + 64]
        ins = nc.scalar.mul(dst, po, rz)
        mk_ev = self.ev_done(ins, "scalar", bank)
        self.a8_free[st % 2] = mk_ev
        return mk_ev


_CACHE = {}


def _build_program():
    if "nc" not in _CACHE:
        _CACHE["nc"] = Prog().build()
    return _CACHE["nc"]


def kernel(L0_attn_x, L0_attn_w_qkv, L0_attn_w_out, L0_ffn_Wr, L0_ffn_br,
           L0_ffn_W1, L0_ffn_b1, L0_ffn_W2, L0_ffn_b2):
    x = np.ascontiguousarray(np.asarray(L0_attn_x, dtype=np.float32))
    wqkv = np.ascontiguousarray(np.asarray(L0_attn_w_qkv, dtype=np.float32))
    wout = np.ascontiguousarray(np.asarray(L0_attn_w_out, dtype=np.float32))
    Wr = np.ascontiguousarray(np.asarray(L0_ffn_Wr, dtype=np.float32))
    br = np.ascontiguousarray(np.asarray(L0_ffn_br, dtype=np.float32))
    W1 = np.ascontiguousarray(np.asarray(L0_ffn_W1, dtype=np.float32))
    b1 = np.ascontiguousarray(np.asarray(L0_ffn_b1, dtype=np.float32))
    W2 = np.ascontiguousarray(np.asarray(L0_ffn_W2, dtype=np.float32))
    b2 = np.ascontiguousarray(np.asarray(L0_ffn_b2, dtype=np.float32))

    w1a = np.ascontiguousarray(W1.reshape(E, IC, 128, FT, 128).transpose(0, 3, 2, 1, 4))
    w2a = W2
    qpart = wqkv[0:D].reshape(NH, HD, D)
    kpart = wqkv[D:2 * D].reshape(NH, HD, D)
    qk = np.concatenate([qpart, kpart], axis=1)
    qkwa = np.ascontiguousarray(qk.reshape(NH, 128, IC, 128).transpose(0, 3, 2, 1))
    vpart = wqkv[2 * D:3 * D].reshape(NH // 2, 128, D)
    vwa = np.ascontiguousarray(vpart.reshape(NH // 2, 128, IC, 128).transpose(0, 3, 2, 1))
    woa = np.ascontiguousarray(wout.T.reshape(IC, 128, D))
    b1a = np.ascontiguousarray(b1.reshape(E, FT, 128).transpose(2, 0, 1).reshape(128, E * FT))
    b2a = np.ascontiguousarray(np.broadcast_to(b2[:, None, :], (E, 128, D)))
    wrx = np.ascontiguousarray(Wr.reshape(IC, 128, E).transpose(1, 0, 2).reshape(128, IC * E))
    bra = np.ascontiguousarray(np.broadcast_to(br[None, :], (128, E)))
    idx = np.arange(128)
    trila = np.where(idx[:, None] >= idx[None, :], 0.0, NEG).astype(np.float32)

    nc = _build_program()
    in_maps = []
    for c in range(NCORE):
        xs = np.ascontiguousarray(x[c * B:(c + 1) * B].reshape(B * (S // 128), 128, D))
        in_maps.append({
            "x": xs, "w1a": w1a, "w2a": w2a, "qkwa": qkwa, "vwa": vwa,
            "woa": woa, "b1a": b1a, "b2a": b2a, "wrx": wrx, "bra": bra,
            "trila": trila,
        })
    res = run_bass_kernel_spmd(nc, in_maps, core_ids=list(range(NCORE)))
    if _DBG:
        global _LAST_RES
        _LAST_RES = res
    outs = [res.results[c]["y"].reshape(B, S, D) for c in range(NCORE)]
    return np.concatenate(outs, axis=0)


# version 06
# speedup vs baseline: 1.1372x; 1.1372x over previous
import numpy as np
import concourse.bass as bass
import concourse.mybir as mybir
from concourse.bass_utils import run_bass_kernel_spmd

F32 = mybir.dt.float32
BF16 = mybir.dt.bfloat16
AF = mybir.ActivationFunctionType
ALU = mybir.AluOpType
AX = mybir.AxisListType

NCORE = 8
B = 2
S = 2048
D = 1024
NH = 16
HD = 64
E = 8
DFF = 4096
NTILE = 32
NTT = 4
TTK = 8
FT = 32
IC = 8
NEG = -1.0e30

_STAGE = 3
_DBG = 0
_NE = E
_NTTR = NTT
_AB = list(range(B))
_LAST_RES = None


class Prog:
    def __init__(self):
        nc = bass.Bass("TRN2", target_bir_lowering=False)
        self.nc = nc

        def dram(name, shape, dtype, kind):
            return nc.dram_tensor(name, shape, dtype, kind=kind).ap()

        self.d_x = dram("x", [NTILE, 128, D], F32, "ExternalInput")
        self.d_y = dram("y", [NTILE, 128, D], F32, "ExternalOutput")
        self.d_w1 = dram("w1a", [E, FT, 128, IC, 128], F32, "ExternalInput")
        self.d_w2 = dram("w2a", [E, DFF, D], F32, "ExternalInput")
        self.d_qkw = dram("qkwa", [NH, 128, IC, 128], F32, "ExternalInput")
        self.d_vw = dram("vwa", [NH // 2, 128, IC, 128], F32, "ExternalInput")
        self.d_wo = dram("woa", [IC, 128, D], F32, "ExternalInput")
        self.d_b1 = dram("b1a", [128, E * FT], F32, "ExternalInput")
        self.d_b2 = dram("b2a", [E, 128, D], F32, "ExternalInput")
        self.d_wrx = dram("wrx", [128, IC * E], F32, "ExternalInput")
        self.d_br = dram("bra", [128, E], F32, "ExternalInput")
        self.d_tril = dram("trila", [128, 128], F32, "ExternalInput")
        if _DBG:
            self.d_dbg_c = dram("dbg_c", [128, NTILE * E], F32, "ExternalOutput")
            self.d_dbg_st = dram("dbg_st", [128, NTILE * 2], F32, "ExternalOutput")
            self.d_dbg_rt = dram("dbg_rt", [128, TTK * 32], F32, "ExternalOutput")
            self.d_dbg_h1 = dram("dbg_h1", [128, 8192], BF16, "ExternalOutput")
            self.d_dbg_xnt = dram("dbg_xnt", [128, 8192], BF16, "ExternalOutput")
            self.d_dbg_x = dram("dbg_x", [128, 4096], BF16, "ExternalOutput")
            self.d_dbg_w1 = dram("dbg_w1", [128, 4096], BF16, "ExternalOutput")
            self.d_dbg_yal = dram("dbg_yal", [128, 4096], BF16, "ExternalOutput")
            self.d_dbg_qk = dram("dbg_qk", [128, 4096], BF16, "ExternalOutput")
            self.d_dbg_pr = dram("dbg_pr", [128, 4096], BF16, "ExternalOutput")
            self.d_dbg_hw = dram("dbg_hw", [128, 4096], BF16, "ExternalOutput")
            self.d_dbg_v0 = dram("dbg_v0", [128, 2048], BF16, "ExternalOutput")

        sb = nc.alloc_sbuf_tensor
        self.x_sb = sb("x_sb", [128, NTILE * D], BF16)
        self.stat = sb("stat", [128, NTILE * 2], F32)
        self.c_sb = sb("c_sb", [128, NTILE * E], F32)
        self.b1_sb = sb("b1_sb", [128, E * FT], F32)
        self.br_sb = sb("br_sb", [128, E], F32)
        self.tril_sb = sb("tril_sb", [128, 128], F32)
        self.wr_sb = sb("wr_sb", [128, IC * E], BF16)
        self.sm = sb("sm", [128, 32], F32)
        self.rt = sb("rt", [128, TTK * 32], F32)
        self.a8 = sb("a8", [128, 64], F32)
        self.dump = sb("dump", [128, D], F32)
        self.ystage = sb("ystage", [128, 2 * D], F32)
        self.xn_nat = sb("xn_nat", [128, 2 * D], BF16)
        self.p0reg = sb("p0reg", [128, 2 * D], F32)
        self.b2buf = sb("b2buf", [128, 2 * D], F32)
        self.w1buf = sb("w1buf", [128, 4 * 1024], BF16)
        self.w2buf = sb("w2buf", [128, 8 * 512], BF16)
        self.arena = sb("arena", [128, 45120], BF16)

        self.yal = self.ystage.bitcast(BF16)
        self.hw = self.b2buf.bitcast(BF16)
        self.wo_a = self.p0reg.bitcast(BF16)

        self.FXNT = 0
        self.FH1 = 8192
        self.AXNT = 0
        self.AOT = 16384
        self.AQKN = 32768
        self.AQT = 34880
        self.AKT = 36928
        self.APR = 38976
        self.APT = 43072

        self.ps = nc.alloc_psum_tensor("ps", [128, 4096], F32)

        self.sem_h = {}
        self.sem_c = {}
        for s_ in ("mm", "sc", "vx", "gw", "tp", "sy"):
            self.sem_h[s_] = nc.alloc_semaphore(s_)
            self.sem_c[s_] = 0
        self.waited = {e: {} for e in ("tensor", "scalar", "vector", "sync", "gpsimd")}
        self.eng = None
        self.bank_writer = [None] * 8
        self.bank_readers = [[] for _ in range(8)]

    def inc(self, ins, sem, n=1):
        ins.then_inc(self.sem_h[sem], n)
        self.sem_c[sem] += n
        return (sem, self.sem_c[sem])

    def wait(self, engname, sem, cnt):
        if cnt <= 0:
            return
        if self.waited[engname].get(sem, -1) >= cnt:
            return
        self.eng[engname].wait_ge(self.sem_h[sem], cnt)
        self.waited[engname][sem] = cnt

    def wmk(self, engname, marks):
        for m in marks:
            if m is not None:
                self.wait(engname, m[0], m[1])

    def vsync(self, ins):
        mk = self.inc(ins, "vx")
        self.wmk("vector", [mk])
        return mk

    def ssync(self, ins):
        mk = self.inc(ins, "sc")
        self.wmk("scalar", [mk])
        return mk

    def t_banks(self, banks):
        for b in banks:
            for mk in self.bank_readers[b]:
                self.wait("tensor", mk[0], mk[1])
            self.bank_readers[b] = []

    def t_close(self, ins, banks):
        mk = self.inc(ins, "mm")
        for b in banks:
            self.bank_writer[b] = mk
        return mk

    def ev_wait(self, engname, b):
        w = self.bank_writer[b]
        assert w is not None, f"bank {b} never written"
        self.wait(engname, w[0], w[1])

    def ev_done(self, ins, engname, banks):
        sem = {"scalar": "sc", "vector": "vx"}[engname]
        mk = self.inc(ins, sem)
        if isinstance(banks, int):
            banks = [banks]
        for b in banks:
            self.bank_readers[b].append(mk)
        return mk

    def build(self):
        nc = self.nc
        with nc.Block() as block:
            @block.sync
            def _(_sync):
                self.eng = {
                    "tensor": nc.tensor, "scalar": nc.scalar,
                    "vector": nc.vector, "sync": nc.sync, "gpsimd": nc.gpsimd,
                }
                self.emit()
        return nc

    def stats_tile(self, t, extra_wait=()):
        nc = self.nc
        slot = t % 2
        base = slot * 16
        xs = self.x_sb.ap()[:, t * D:(t + 1) * D]
        st_ = self.stat.ap()
        smv = self.sm.ap()
        c_s1 = smv[:, base + 0:base + 1]
        c_ss = smv[:, base + 1:base + 2]
        c_mean = smv[:, base + 2:base + 3]
        c_tv = smv[:, base + 3:base + 4]
        c_m2 = smv[:, base + 4:base + 5]
        c_v2 = smv[:, base + 5:base + 6]
        c_ln = smv[:, base + 6:base + 7]
        c_tmp = smv[:, base + 7:base + 8]
        c_v2a = smv[:, base + 8:base + 9]
        self.wmk("vector", list(extra_wait))
        if self.sm_slot_free[slot] is not None:
            self.wmk("vector", [self.sm_slot_free[slot]])
        nc.vector.reduce_sum(c_s1, xs, axis=AX.X)
        nc.vector.tensor_mul(self.dump.ap()[:, 0:D], xs, xs)
        nc.vector.reduce_sum(c_ss, self.dump.ap()[:, 0:D], axis=AX.X)
        nc.vector.tensor_scalar(c_mean, c_s1, 1.0 / D, None, ALU.mult)
        nc.vector.tensor_scalar(c_tv, c_ss, 1.0 / D, None, ALU.mult)
        self.vsync(nc.vector.tensor_mul(c_m2, c_mean, c_mean))
        self.vsync(nc.vector.tensor_sub(c_v2a, c_tv, c_m2))
        ins = nc.vector.tensor_scalar(c_v2, c_v2a, 1e-5, None, ALU.add)
        mk_v = self.inc(ins, "vx")
        self.wmk("scalar", [mk_v])
        nc.scalar.activation(c_ln, c_v2, AF.Ln)
        ins = nc.scalar.activation(st_[:, 2 * t:2 * t + 1], c_ln, AF.Exp, scale=-0.5)
        mk_s = self.inc(ins, "sc")
        self.wmk("vector", [mk_s])
        self.vsync(nc.vector.tensor_mul(c_tmp, c_mean, st_[:, 2 * t:2 * t + 1]))
        ins = nc.vector.tensor_scalar(st_[:, 2 * t + 1:2 * t + 2], c_tmp, -1.0,
                                      None, ALU.mult)
        mk_done = self.inc(ins, "vx")
        self.sm_slot_free[slot] = mk_done
        self.stat_ready[t] = mk_done

    def emit(self):
        nc = self.nc
        self.sm_slot_free = [None, None]
        self.stat_ready = [None] * NTILE
        self.x_final = [None] * NTILE
        self.w1_slot_free = [None] * 4
        self.w2_slot_free = [None] * 8
        self.b2_slot_free = [None, None]
        self.xn_slot_free = [None, None]
        self.yst_slot_free = [None, None]
        self.a8_free = {}
        self.c_ready = {}
        self.rt_free = None
        self.xnt_used = None
        self.axnt_used = None
        self.aot_used = None
        self.probs_free = [None, None]
        self.probsT_used = None
        self.onat_free = None
        self.vsb_free = None
        self.qkn_free = None
        self.qkt_used = None
        self.qkw_free = [None, None]
        self.vw_free = [None, None]
        self._last_A = None
        self._last_B = None
        self._last_lg = None
        self.mk_xnt_cur = None

        ins = nc.sync.dma_start(out=self.b1_sb.ap()[:], in_=self.d_b1)
        self.inc(ins, "sy", 16)
        ins = nc.sync.dma_start(out=self.br_sb.ap()[:], in_=self.d_br)
        self.inc(ins, "sy", 16)
        ins = nc.sync.dma_start(out=self.tril_sb.ap()[:], in_=self.d_tril)
        self.mk_consts = self.inc(ins, "sy", 16)
        ins = nc.gpsimd.dma_start(out=self.wr_sb.ap()[:], in_=self.d_wrx)
        self.mk_wr = self.inc(ins, "gw", 16)

        for t in range(NTILE):
            ins = nc.gpsimd.dma_start(out=self.x_sb.ap()[:, t * D:(t + 1) * D],
                                      in_=self.d_x[t])
            self.x_final[t] = self.inc(ins, "gw", 16)

        self.ffn_call()
        if _STAGE >= 2:
            self.attn()
        if _STAGE >= 3:
            self.ffn_call()

        if _DBG:
            for s_ in ("mm", "sc", "vx", "gw", "tp"):
                self.wait("sync", s_, self.sem_c[s_])
            ar = self.arena.ap()
            ins = nc.sync.dma_start(out=self.d_dbg_c, in_=self.c_sb.ap()[:])
            self.inc(ins, "sy", 16)
            ins = nc.sync.dma_start(out=self.d_dbg_st, in_=self.stat.ap()[:])
            self.inc(ins, "sy", 16)
            ins = nc.sync.dma_start(out=self.d_dbg_rt, in_=self.rt.ap()[:])
            self.inc(ins, "sy", 16)
            ins = nc.sync.dma_start(out=self.d_dbg_h1,
                                    in_=ar[:, self.FH1:self.FH1 + 8192])
            self.inc(ins, "sy", 16)
            ins = nc.sync.dma_start(out=self.d_dbg_xnt,
                                    in_=ar[:, self.FXNT:self.FXNT + 8192])
            self.inc(ins, "sy", 16)
            ins = nc.sync.dma_start(out=self.d_dbg_x,
                                    in_=self.x_sb.ap()[:, 0:4096])
            self.inc(ins, "sy", 16)
            ins = nc.sync.dma_start(out=self.d_dbg_w1, in_=self.w1buf.ap()[:])
            self.inc(ins, "sy", 16)
            ins = nc.sync.dma_start(out=self.d_dbg_yal, in_=self.yal.ap()[:])
            self.inc(ins, "sy", 16)
            ins = nc.sync.dma_start(
                out=self.d_dbg_qk, in_=ar[:, self.AQT:self.AQT + 4096])
            self.inc(ins, "sy", 16)
            ins = nc.sync.dma_start(
                out=self.d_dbg_pr, in_=ar[:, self.APR:self.APR + 4096])
            self.inc(ins, "sy", 16)
            ins = nc.sync.dma_start(out=self.d_dbg_hw, in_=self.hw.ap()[:])
            self.inc(ins, "sy", 16)

        if _STAGE >= 2:
            self.wmk("scalar", [self.onat_free])
        for t in range(NTILE):
            slot = t % 2
            ys = self.ystage.ap()[:, slot * D:(slot + 1) * D]
            self.wmk("scalar", [self.x_final[t], self.yst_slot_free[slot]])
            ins = nc.scalar.copy(ys, self.x_sb.ap()[:, t * D:(t + 1) * D])
            mk = self.inc(ins, "sc")
            self.wmk("sync", [mk])
            ins = nc.sync.dma_start(out=self.d_y[t], in_=ys)
            self.yst_slot_free[slot] = self.inc(ins, "sy", 16)
        self.wait("sync", "sy", self.sem_c["sy"])

    def ffn_call(self):
        for t in range(NTILE):
            self.stats_tile(t, extra_wait=[self.x_final[t]])
        for tt in range(_NTTR):
            self.ffn_chunk(tt)

    def router_chunk(self, tt):
        nc = self.nc
        ar = self.arena.ap()
        rt = self.rt.ap()
        ps = self.ps.ap()
        self.t_banks([7])
        self.wmk("tensor", [self.mk_wr, self.mk_xnt_cur])
        last = None
        for k in range(TTK):
            for ic in range(IC):
                lhsT = ar[:, self.FXNT + ic * 1024 + k * 128:
                          self.FXNT + ic * 1024 + k * 128 + 128]
                rhs = self.wr_sb.ap()[:, ic * E:(ic + 1) * E]
                last = nc.tensor.matmul(ps[:, 3584 + k * E:3584 + (k + 1) * E],
                                        lhsT, rhs, start=(ic == 0),
                                        stop=(ic == IC - 1))
        gm = self.t_close(last, [7])
        self.ev_wait("vector", 7)
        self.wmk("vector", [self.mk_consts, self.rt_free])
        last_v = None
        for k in range(TTK):
            last_v = nc.vector.tensor_add(rt[:, k * 32:k * 32 + E],
                                          ps[:, 3584 + k * E:3584 + (k + 1) * E],
                                          self.br_sb.ap()[:])
        mk_evac = self.inc(last_v, "vx")
        self.bank_readers[7].append(mk_evac)
        self.wmk("vector", [mk_evac])
        for k in range(TTK):
            last_v = nc.vector.reduce_max(rt[:, k * 32 + 24:k * 32 + 25],
                                          rt[:, k * 32:k * 32 + E],
                                          axis=AX.X, negate=True)
        mk_nm = self.inc(last_v, "vx")
        self.wmk("scalar", [mk_nm])
        last_s = None
        for k in range(TTK):
            nc.scalar.activation(rt[:, k * 32 + 8:k * 32 + 16],
                                 rt[:, k * 32:k * 32 + E], AF.Exp,
                                 bias=rt[:, k * 32 + 24:k * 32 + 25],
                                 accum_out=rt[:, k * 32 + 25:k * 32 + 26])
            last_s = nc.scalar.sign(rt[:, k * 32 + 16:k * 32 + 24],
                                    rt[:, k * 32:k * 32 + E],
                                    bias=rt[:, k * 32 + 24:k * 32 + 25])
        mk_s = self.inc(last_s, "sc")
        self.wmk("vector", [mk_s])
        for k in range(TTK):
            last_v = nc.vector.tensor_scalar(rt[:, k * 32 + 26:k * 32 + 27],
                                             rt[:, k * 32 + 25:k * 32 + 26],
                                             2.0, None, ALU.mult)
        self.wmk("vector", [self.inc(last_v, "vx")])
        for k in range(TTK):
            last_v = nc.vector.reciprocal(rt[:, k * 32 + 27:k * 32 + 28],
                                          rt[:, k * 32 + 26:k * 32 + 27])
        self.wmk("vector", [self.inc(last_v, "vx")])
        for k in range(TTK):
            t = tt * TTK + k
            last_v = nc.vector.tensor_scalar(
                self.c_sb.ap()[:, t * E:(t + 1) * E],
                rt[:, k * 32 + 16:k * 32 + 24], 1.0,
                rt[:, k * 32 + 27:k * 32 + 28], ALU.add, ALU.mult)
        mk_c = self.inc(last_v, "vx")
        self.c_ready[tt] = mk_c
        self.rt_free = mk_c

    def ffn_chunk(self, tt):
        nc = self.nc
        ar = self.arena.ap()
        mk_xnt = None
        for k in range(TTK):
            t = tt * TTK + k
            slot = t % 2
            xn = self.xn_nat.ap()[:, slot * D:(slot + 1) * D]
            self.wmk("vector", [self.stat_ready[t], self.xn_slot_free[slot]])
            st_ = self.stat.ap()
            ins = nc.vector.tensor_scalar(xn, self.x_sb.ap()[:, t * D:(t + 1) * D],
                                          st_[:, 2 * t:2 * t + 1],
                                          st_[:, 2 * t + 1:2 * t + 2],
                                          ALU.mult, ALU.add)
            mk = self.inc(ins, "vx")
            self.wmk("sync", [mk, self.xnt_used])
            dst = ar[:, self.FXNT:self.FXNT + IC * 1024].rearrange(
                "p (ic ttk c) -> p ic ttk c", ic=IC, ttk=TTK)[:, :, k, :]
            ins = nc.sync.dma_start(out=dst, in_=xn, transpose=True)
            mk_xnt = self.inc(ins, "tp", 16)
            self.xn_slot_free[slot] = mk_xnt
        self.mk_xnt_cur = mk_xnt

        self.router_chunk(tt)

        def load_w1_pair(e_, ft0):
            s0 = ft0 % 4
            self.wmk("gpsimd", [self.w1_slot_free[s0], self.w1_slot_free[s0 + 1]])
            dst = self.w1buf.ap()[:, s0 * 1024:(s0 + 2) * 1024]
            ins = nc.gpsimd.dma_start(
                out=dst,
                in_=self.d_w1[e_, ft0:ft0 + 2].rearrange("a p ic c -> p a ic c"))
            return self.inc(ins, "gw", 16)

        def load_w2_quad(e_, g0):
            hh_, f0 = divmod(g0, FT)
            s4 = g0 % 8
            self.wmk("gpsimd", [self.w2_slot_free[s4 + i] for i in range(4)])
            dst = self.w2buf.ap()[:, s4 * 512:(s4 + 4) * 512]
            ins = nc.gpsimd.dma_start(
                out=dst,
                in_=self.d_w2[e_, f0 * 128:(f0 + 4) * 128,
                              hh_ * 512:(hh_ + 1) * 512].rearrange(
                    "(a p) c -> p a c", p=128))
            return self.inc(ins, "gw", 16)

        for e in range(_NE):
            w1marks = [None] * FT
            for ft0 in (0, 2):
                mk = load_w1_pair(e, ft0)
                w1marks[ft0] = mk
                w1marks[ft0 + 1] = mk
            w2marks = {}
            b2s = e % 2
            mk_b2 = None

            h1_ready = [None] * FT
            for ft in range(FT):
                self.wmk("tensor", [w1marks[ft]])
                if ft == 0:
                    self.wmk("tensor", [mk_xnt])
                w1sl = self.w1buf.ap()[:, (ft % 4) * 1024:(ft % 4 + 1) * 1024]
                for nh2 in range(2):
                    bank = (ft * 2 + nh2) % 2
                    self.t_banks([bank])
                    pslice = self.ps.ap()[:, bank * 512:(bank + 1) * 512]
                    last = None
                    for ic in range(IC):
                        rhs = ar[:, self.FXNT + ic * 1024 + nh2 * 512:
                                 self.FXNT + ic * 1024 + nh2 * 512 + 512]
                        last = nc.tensor.matmul(pslice, w1sl[:, ic * 128:(ic + 1) * 128],
                                                rhs, start=(ic == 0), stop=(ic == IC - 1))
                    gm = self.t_close(last, [bank])
                    if nh2 == 1:
                        self.w1_slot_free[ft % 4] = gm
                        if ft == FT - 1:
                            self._last_A = gm
                        if ft % 2 == 1 and ft + 3 < FT:
                            mk = load_w1_pair(e, ft + 3)
                            w1marks[ft + 3] = mk
                            w1marks[ft + 4] = mk
                        if ft == 7 or ft == 15:
                            g0 = 0 if ft == 7 else 4
                            mk = load_w2_quad(e, g0)
                            for i in range(4):
                                w2marks[g0 + i] = mk
                        elif ft == 23:
                            self.wmk("gpsimd", [self.b2_slot_free[b2s]])
                            ins = nc.gpsimd.dma_start(
                                out=self.b2buf.ap()[:, b2s * D:(b2s + 1) * D],
                                in_=self.d_b2[e])
                            mk_b2 = self.inc(ins, "gw", 16)
                    self.ev_wait("scalar", bank)
                    dst = ar[:, self.FH1 + ft * 1024 + nh2 * 512:
                             self.FH1 + ft * 1024 + nh2 * 512 + 512]
                    ins = nc.scalar.activation(
                        dst, pslice, AF.Relu,
                        bias=self.b1_sb.ap()[:, e * FT + ft:e * FT + ft + 1])
                    mk = self.ev_done(ins, "scalar", bank)
                    if nh2 == 1:
                        h1_ready[ft] = mk

            for hh in range(2):
                for f in range(FT):
                    g = hh * FT + f
                    s_ = g % 8
                    self.wmk("tensor", [w2marks[g], h1_ready[f]])
                    w2sl = self.w2buf.ap()[:, s_ * 512:(s_ + 1) * 512]
                    if f == 0:
                        self.t_banks(list(range(8)))
                    last = None
                    for t8 in range(TTK):
                        lhs = ar[:, self.FH1 + f * 1024 + t8 * 128:
                                 self.FH1 + f * 1024 + t8 * 128 + 128]
                        pslice = self.ps.ap()[:, t8 * 512:(t8 + 1) * 512]
                        last = nc.tensor.matmul(pslice, lhs, w2sl,
                                                start=(f == 0), stop=(f == FT - 1))
                    gm = self.t_close(last, list(range(8)) if f == FT - 1 else [])
                    self.w2_slot_free[s_] = gm
                    if g % 4 == 3 and g + 5 + 3 < 2 * FT:
                        mk = load_w2_quad(e, g + 5)
                        for i in range(4):
                            w2marks[g + 5 + i] = mk
                    if f == FT - 1:
                        self._last_B = gm
                last_v = None
                for t8 in range(TTK):
                    t = tt * TTK + t8
                    self.ev_wait("vector", t8)
                    self.wmk("vector", [mk_b2, self.c_ready[tt]])
                    pslice = self.ps.ap()[:, t8 * 512:(t8 + 1) * 512]
                    xsl = self.x_sb.ap()[:, t * D + hh * 512:t * D + hh * 512 + 512]
                    ccol = self.c_sb.ap()[:, t * E + e:t * E + e + 1]
                    nc.vector.scalar_tensor_tensor(xsl, pslice, ccol, xsl,
                                                   ALU.mult, ALU.add)
                    b2sl = self.b2buf.ap()[:, b2s * D + hh * 512:b2s * D + hh * 512 + 512]
                    ins = nc.vector.scalar_tensor_tensor(xsl, b2sl, ccol, xsl,
                                                         ALU.mult, ALU.add)
                    mk = self.ev_done(ins, "vector", t8)
                    last_v = mk
                    if hh == 1 and e == _NE - 1:
                        self.x_final[t] = mk
                if hh == 1:
                    self.b2_slot_free[b2s] = last_v

        self.xnt_used = self._last_A

    def attn(self):
        nc = self.nc
        for t in range(NTILE):
            self.stats_tile(t, extra_wait=[self.x_final[t]])

        self.wmk("gpsimd", self.w1_slot_free + self.w2_slot_free)
        ins = nc.gpsimd.dma_start(
            out=self.wo_a.ap()[:, 0:4096].rearrange("p (a d) -> p a d", d=D),
            in_=self.d_wo[0:4].rearrange("a p d -> p a d"))
        mk_wo0 = self.inc(ins, "gw", 16)
        ins = nc.gpsimd.dma_start(
            out=self.w1buf.ap()[:, 0:4096].rearrange("p (a d) -> p a d", d=D),
            in_=self.d_wo[4:8].rearrange("a p d -> p a d"))
        mk_wo1 = self.inc(ins, "gw", 16)
        self.wo_marks = [mk_wo0, mk_wo1]
        self.axnt_used = self._last_B

        self._attn_first = True
        for b in _AB:
            self.attn_batch(b)
            self._attn_first = False

        self.xnt_used = self._attn_last_mm
        for s_ in range(4):
            self.w1_slot_free[s_] = self._attn_last_mm
        for s_ in range(8):
            self.w2_slot_free[s_] = self._attn_last_mm
        self.b2_slot_free = [self._attn_last_mm, self._attn_last_mm]

    def attn_batch(self, b):
        nc = self.nc
        ar = self.arena.ap()
        mk_xntb = None
        for k in range(16):
            t = b * 16 + k
            slot = t % 2
            xn = self.xn_nat.ap()[:, slot * D:(slot + 1) * D]
            self.wmk("vector", [self.stat_ready[t], self.x_final[t],
                                self.xn_slot_free[slot]])
            st_ = self.stat.ap()
            ins = nc.vector.tensor_scalar(xn, self.x_sb.ap()[:, t * D:(t + 1) * D],
                                          st_[:, 2 * t:2 * t + 1],
                                          st_[:, 2 * t + 1:2 * t + 2],
                                          ALU.mult, ALU.add)
            mk = self.inc(ins, "vx")
            self.wmk("sync", [mk, self.xnt_used, self.axnt_used])
            dst = ar[:, self.AXNT:self.AXNT + IC * 2048].rearrange(
                "p (ic kk c) -> p ic kk c", ic=IC, kk=16)[:, :, k, :]
            ins = nc.sync.dma_start(out=dst, in_=xn, transpose=True)
            mk_xntb = self.inc(ins, "tp", 16)
            self.xn_slot_free[slot] = mk_xntb
        self.xnt_used = None

        last_mm = None
        for hp in range(NH // 2):
            vs = hp % 2
            self.wmk("gpsimd", [self.vw_free[vs]])
            if hp == 0 and self._attn_first:
                self.wmk("gpsimd", [self.b2_slot_free[0], self.b2_slot_free[1]])
            ins = nc.gpsimd.dma_start(
                out=self.hw.ap()[:, 2048 + vs * 1024:2048 + (vs + 1) * 1024],
                in_=self.d_vw[hp].rearrange("p ic c -> p (ic c)"))
            mk_vw = self.inc(ins, "gw", 16)
            qk_marks = []
            for hh2 in range(2):
                h = 2 * hp + hh2
                qs = h % 2
                self.wmk("gpsimd", [self.qkw_free[qs]])
                ins = nc.gpsimd.dma_start(
                    out=self.hw.ap()[:, qs * 1024:(qs + 1) * 1024],
                    in_=self.d_qkw[h].rearrange("p ic c -> p (ic c)"))
                qk_marks.append(self.inc(ins, "gw", 16))

            v_ev = None
            for kt in range(16):
                bank = 7
                colA = 3584 + (kt % 2) * 128
                self.t_banks([bank])
                self.wmk("tensor", [mk_xntb, mk_vw])
                pv = self.ps.ap()[:, colA:colA + 128]
                last = None
                for ic in range(IC):
                    lhs = ar[:, self.AXNT + ic * 2048 + kt * 128:
                             self.AXNT + ic * 2048 + kt * 128 + 128]
                    rhs = self.hw.ap()[:, 2048 + vs * 1024 + ic * 128:
                                       2048 + vs * 1024 + (ic + 1) * 128]
                    last = nc.tensor.matmul(pv, lhs, rhs, start=(ic == 0),
                                            stop=(ic == IC - 1))
                gm = self.t_close(last, [bank])
                if kt == 15:
                    self.vw_free[vs] = gm
                self.ev_wait("scalar", bank)
                self.wmk("scalar", [self.vsb_free])
                ins = nc.scalar.copy(self.yal.ap()[:, kt * 128:(kt + 1) * 128], pv)
                v_ev = self.ev_done(ins, "scalar", bank)
            mk_vsb = v_ev
            self.vsb_free = None
            if _DBG and b == 1 and hp == 0:
                self.wmk("sync", [mk_vsb])
                ins = nc.sync.dma_start(out=self.d_dbg_v0,
                                        in_=self.yal.ap()[:, 0:2048])
                self.inc(ins, "sy", 16)

            onat_evs = []
            for hh2 in range(2):
                h = 2 * hp + hh2
                qs = h % 2
                qk_ev = None
                for kt in range(16):
                    bank = 6
                    colA = 3072 + (kt % 2) * 128
                    self.t_banks([bank])
                    self.wmk("tensor", [qk_marks[hh2]])
                    pv = self.ps.ap()[:, colA:colA + 128]
                    last = None
                    for ic in range(IC):
                        lhs = ar[:, self.AXNT + ic * 2048 + kt * 128:
                                 self.AXNT + ic * 2048 + kt * 128 + 128]
                        rhs = self.hw.ap()[:, qs * 1024 + ic * 128:
                                           qs * 1024 + (ic + 1) * 128]
                        last = nc.tensor.matmul(pv, lhs, rhs, start=(ic == 0),
                                                stop=(ic == IC - 1))
                    gm = self.t_close(last, [bank])
                    if kt == 15:
                        self.qkw_free[qs] = gm
                    self.ev_wait("scalar", bank)
                    self.wmk("scalar", [self.qkn_free])
                    nc.scalar.mul(ar[:, self.AQKN + kt * 128:self.AQKN + kt * 128 + 64],
                                  pv[:, 0:64], 0.125)
                    ins = nc.scalar.copy(
                        ar[:, self.AQKN + kt * 128 + 64:self.AQKN + (kt + 1) * 128],
                        pv[:, 64:128])
                    qk_ev = self.ev_done(ins, "scalar", bank)
                self.wmk("sync", [qk_ev, self.qkt_used])
                dstq = ar[:, self.AQT:self.AQT + 2048].rearrange("p (a c) -> p a c", c=128)
                ins = nc.sync.dma_start(out=dstq, in_=ar[:, self.AQKN:self.AQKN + 2048],
                                        transpose=True)
                self.inc(ins, "tp", 16)
                dstk = ar[:, self.AKT:self.AKT + 2048].rearrange("p (a c) -> p a c", c=128)
                ins = nc.sync.dma_start(
                    out=dstk, in_=ar[:, self.AQKN + 64:self.AQKN + 64 + 2048],
                    transpose=True)
                mk_qkt = self.inc(ins, "tp", 16)
                self.qkn_free = mk_qkt

                for st in range(16):
                    ev = self.attn_stile(h, hh2, st, mk_qkt, mk_vsb)
                    onat_evs.append(ev)
                self.qkt_used = self._last_lg

            self.wmk("sync", onat_evs[-32:])
            if not self._attn_first:
                self.wmk("sync", [self.aot_used])
            dst = ar[:, self.AOT + hp * 2048:self.AOT + (hp + 1) * 2048].rearrange(
                "p (a c) -> p a c", c=128)
            ins = nc.sync.dma_start(out=dst, in_=self.yal.ap()[:, 2048:4096],
                                    transpose=True)
            self.onat_free = self.inc(ins, "tp", 16)
            self.oT_marks = getattr(self, "oT_marks", [])
            if hp == 0:
                self.oT_marks = []
            self.oT_marks.append(self.onat_free)
            self.vsb_free = self.onat_free

        self.wmk("tensor", self.wo_marks + self.oT_marks)
        for k in range(16):
            t = b * 16 + k
            for no in range(2):
                bank = no
                self.t_banks([bank])
                pslice = self.ps.ap()[:, bank * 512:(bank + 1) * 512]
                last = None
                for ic in range(IC):
                    lhs = ar[:, self.AOT + ic * 2048 + k * 128:
                             self.AOT + ic * 2048 + k * 128 + 128]
                    if ic < 4:
                        rhs = self.wo_a.ap()[:, ic * 1024 + no * 512:
                                             ic * 1024 + no * 512 + 512]
                    else:
                        rhs = self.w1buf.ap()[:, (ic - 4) * 1024 + no * 512:
                                              (ic - 4) * 1024 + no * 512 + 512]
                    last = nc.tensor.matmul(pslice, lhs, rhs, start=(ic == 0),
                                            stop=(ic == IC - 1))
                gm = self.t_close(last, [bank])
                last_mm = gm
                self.ev_wait("vector", bank)
                xsl = self.x_sb.ap()[:, t * D + no * 512:t * D + no * 512 + 512]
                ins = nc.vector.tensor_add(xsl, xsl, pslice)
                mk = self.ev_done(ins, "vector", bank)
                if no == 1:
                    self.x_final[t] = mk
        self.axnt_used = last_mm
        self.aot_used = last_mm
        self._attn_last_mm = last_mm

    def attn_stile(self, h, hh2, st, mk_qkt, mk_vsb):
        nc = self.nc
        ar = self.arena.ap()
        ps = self.ps.ap()
        tv = (st + 1) * 128
        R0 = 1536 * (st % 2)
        sl = (st % 2) * 32
        a = self.a8.ap()
        nm = [a[:, sl + 0:sl + 1], a[:, sl + 1:sl + 2]]
        Zc = [a[:, sl + 4:sl + 5], a[:, sl + 5:sl + 6]]
        fc = [a[:, sl + 6:sl + 7], a[:, sl + 7:sl + 8]]
        nmt = a[:, sl + 8:sl + 9]
        Zt = a[:, sl + 10:sl + 11]
        rz = a[:, sl + 11:sl + 12]
        sc0 = a[:, sl + 12:sl + 13]
        sc1 = a[:, sl + 13:sl + 14]
        d0 = a[:, sl + 14:sl + 15]
        d1 = a[:, sl + 15:sl + 16]

        widths = [tv] if tv <= 1536 else [1536, tv - 1536]
        pslot = st % 2
        probs = ar[:, self.APR + pslot * 2048:self.APR + pslot * 2048 + 2048]
        self.wmk("scalar", [self.probs_free[pslot]])
        self.wmk("vector", [self.a8_free.get(st % 2)])

        exp_marks = []
        for c, w in enumerate(widths):
            t0 = c * 1536
            lg_banks = []
            nblk = (w + 511) // 512
            for tb in range(nblk):
                bw = min(512, w - tb * 512)
                col = R0 + tb * 512
                bank = col // 512
                self.t_banks([bank])
                self.wmk("tensor", [mk_qkt])
                lhs = ar[0:64, self.AQT + st * 128:self.AQT + (st + 1) * 128]
                rhs = ar[0:64, self.AKT + t0 + tb * 512:self.AKT + t0 + tb * 512 + bw]
                ins = nc.tensor.matmul(ps[:, col:col + bw], lhs, rhs,
                                       start=True, stop=True)
                gm = self.t_close(ins, [bank])
                lg_banks.append(bank)
                self._last_lg = gm
            diag0 = st * 128
            if t0 <= diag0 < t0 + w:
                lc = R0 + (diag0 - t0)
                bank = lc // 512
                self.ev_wait("vector", bank)
                pv = ps[:, lc:lc + 128]
                ins = nc.vector.tensor_add(pv, pv, self.tril_sb.ap()[:])
                mk_d = self.inc(ins, "vx")
                self.bank_readers[bank].append(mk_d)
                self.wmk("vector", [mk_d])
            for bank in lg_banks:
                self.ev_wait("vector", bank)
            ins = nc.vector.reduce_max(nm[c], ps[:, R0:R0 + w], axis=AX.X,
                                       negate=True)
            mk_v = self.inc(ins, "vx")
            self.wmk("scalar", [mk_v])
            ins = nc.scalar.activation(probs[:, t0:t0 + w], ps[:, R0:R0 + w],
                                       AF.Exp, bias=nm[c], accum_out=Zc[c])
            mk_e = self.inc(ins, "sc")
            exp_marks.append(mk_e)
            for bank in lg_banks:
                self.bank_readers[bank].append(mk_e)

        self.wmk("vector", exp_marks)
        if len(widths) == 1:
            ins = nc.vector.reciprocal(rz, Zc[0])
            mk_n = self.inc(ins, "vx")
            probs_mark = exp_marks[-1]
        else:
            self.vsync(nc.vector.tensor_tensor(nmt, nm[0], nm[1], ALU.min))
            nc.vector.tensor_sub(d0, nmt, nm[0])
            ins = nc.vector.tensor_sub(d1, nmt, nm[1])
            mk_v = self.inc(ins, "vx")
            self.wmk("scalar", [mk_v])
            nc.scalar.activation(fc[0], d0, AF.Exp)
            ins = nc.scalar.activation(fc[1], d1, AF.Exp)
            mk_s = self.inc(ins, "sc")
            self.wmk("vector", [mk_s])
            nc.vector.tensor_scalar_mul(probs[:, 0:1536], probs[:, 0:1536], fc[0])
            ins = nc.vector.tensor_scalar_mul(probs[:, 1536:tv], probs[:, 1536:tv],
                                              fc[1])
            probs_mark = self.inc(ins, "vx")
            nc.vector.tensor_mul(sc0, fc[0], Zc[0])
            self.vsync(nc.vector.tensor_mul(sc1, fc[1], Zc[1]))
            self.vsync(nc.vector.tensor_add(Zt, sc0, sc1))
            ins = nc.vector.reciprocal(rz, Zt)
            mk_n = self.inc(ins, "vx")

        self.wmk("sync", [probs_mark, self.probsT_used])
        dst = ar[:, self.APT:self.APT + tv].rearrange("p (a c) -> p a c", c=128)
        ins = nc.sync.dma_start(out=dst, in_=probs[:, 0:tv], transpose=True)
        mk_t = self.inc(ins, "tp", 16)
        self.probs_free[pslot] = mk_t

        bank = 6
        col = 3072 + hh2 * 64
        self.t_banks([bank])
        self.wmk("tensor", [mk_t, mk_vsb])
        po = ps[:, col:col + 64]
        last = None
        for kt in range(st + 1):
            lhs = ar[:, self.APT + kt * 128:self.APT + (kt + 1) * 128]
            rhs = self.yal.ap()[:, kt * 128 + hh2 * 64:kt * 128 + hh2 * 64 + 64]
            last = nc.tensor.matmul(po, lhs, rhs, start=(kt == 0), stop=(kt == st))
        gm = self.t_close(last, [bank])
        self.probsT_used = gm
        self.ev_wait("scalar", bank)
        self.wmk("scalar", [self.onat_free, mk_n])
        dst = self.yal.ap()[:, 2048 + st * 128 + hh2 * 64:2048 + st * 128 + hh2 * 64 + 64]
        ins = nc.scalar.mul(dst, po, rz)
        mk_ev = self.ev_done(ins, "scalar", bank)
        self.a8_free[st % 2] = mk_ev
        return mk_ev


_CACHE = {}


def _build_program():
    if "nc" not in _CACHE:
        _CACHE["nc"] = Prog().build()
    return _CACHE["nc"]


def kernel(L0_attn_x, L0_attn_w_qkv, L0_attn_w_out, L0_ffn_Wr, L0_ffn_br,
           L0_ffn_W1, L0_ffn_b1, L0_ffn_W2, L0_ffn_b2):
    x = np.ascontiguousarray(np.asarray(L0_attn_x, dtype=np.float32))
    wqkv = np.ascontiguousarray(np.asarray(L0_attn_w_qkv, dtype=np.float32))
    wout = np.ascontiguousarray(np.asarray(L0_attn_w_out, dtype=np.float32))
    Wr = np.ascontiguousarray(np.asarray(L0_ffn_Wr, dtype=np.float32))
    br = np.ascontiguousarray(np.asarray(L0_ffn_br, dtype=np.float32))
    W1 = np.ascontiguousarray(np.asarray(L0_ffn_W1, dtype=np.float32))
    b1 = np.ascontiguousarray(np.asarray(L0_ffn_b1, dtype=np.float32))
    W2 = np.ascontiguousarray(np.asarray(L0_ffn_W2, dtype=np.float32))
    b2 = np.ascontiguousarray(np.asarray(L0_ffn_b2, dtype=np.float32))

    w1a = np.ascontiguousarray(W1.reshape(E, IC, 128, FT, 128).transpose(0, 3, 2, 1, 4))
    w2a = W2
    qpart = wqkv[0:D].reshape(NH, HD, D)
    kpart = wqkv[D:2 * D].reshape(NH, HD, D)
    qk = np.concatenate([qpart, kpart], axis=1)
    qkwa = np.ascontiguousarray(qk.reshape(NH, 128, IC, 128).transpose(0, 3, 2, 1))
    vpart = wqkv[2 * D:3 * D].reshape(NH // 2, 128, D)
    vwa = np.ascontiguousarray(vpart.reshape(NH // 2, 128, IC, 128).transpose(0, 3, 2, 1))
    woa = np.ascontiguousarray(wout.T.reshape(IC, 128, D))
    b1a = np.ascontiguousarray(b1.reshape(E, FT, 128).transpose(2, 0, 1).reshape(128, E * FT))
    b2a = np.ascontiguousarray(np.broadcast_to(b2[:, None, :], (E, 128, D)))
    wrx = np.ascontiguousarray(Wr.reshape(IC, 128, E).transpose(1, 0, 2).reshape(128, IC * E))
    bra = np.ascontiguousarray(np.broadcast_to(br[None, :], (128, E)))
    idx = np.arange(128)
    trila = np.where(idx[:, None] >= idx[None, :], 0.0, NEG).astype(np.float32)

    nc = _build_program()
    in_maps = []
    for c in range(NCORE):
        xs = np.ascontiguousarray(x[c * B:(c + 1) * B].reshape(B * (S // 128), 128, D))
        in_maps.append({
            "x": xs, "w1a": w1a, "w2a": w2a, "qkwa": qkwa, "vwa": vwa,
            "woa": woa, "b1a": b1a, "b2a": b2a, "wrx": wrx, "bra": bra,
            "trila": trila,
        })
    res = run_bass_kernel_spmd(nc, in_maps, core_ids=list(range(NCORE)))
    if _DBG:
        global _LAST_RES
        _LAST_RES = res
    outs = [res.results[c]["y"].reshape(B, S, D) for c in range(NCORE)]
    return np.concatenate(outs, axis=0)
